# version 02 (final)
# baseline (speedup 1.0000x reference)
import numpy as np
import ml_dtypes

import concourse.bass as bass
import concourse.mybir as mybir
from concourse.bass_utils import run_bass_kernel_spmd

F32 = mybir.dt.float32
F16 = mybir.dt.float16
BF16 = mybir.dt.bfloat16
U8 = mybir.dt.uint8
AF = mybir.ActivationFunctionType
ALU = mybir.AluOpType
AX = mybir.AxisListType

D = 2048
NT = 16
NO = 8
RQ = 256
NEXP = 17
NEG = -1.0e30
ARENA = 143360

KB = 1024
DBG = None


def build_program(dbg=None):
    nc = bass.Bass("TRN2", target_bir_lowering=False)

    dx = {}
    def din(name, shape, dt=F32):
        dx[name] = nc.dram_tensor(name, list(shape), dt, kind="ExternalInput").ap()
        return dx[name]

    din("xown", [128, NO * D])
    din("xall", [NT, 128, D])
    din("qidx", [128, NO])
    din("kidx", [128, D], F16)
    din("ident_bf", [128, 128], BF16)
    din("ident_f32", [128, 128])
    din("iota16", [128, 16])
    din("iotaB16", [128, 16])
    din("ones1", [1, 128])
    din("onehots", [16, 16 * 128])
    din("ws_w", [128, 16 * RQ])
    din("wa_w", [128, 16 * RQ])
    din("wo_w", [128, 16 * D])
    din("bo_w", [1, D])
    din("router_w", [128, 16 * 16])
    din("ebias_w", [1, 16])
    din("gu_w", [NEXP, 6, 128, 4096])
    din("wd_w", [NEXP, 128, 6 * D])
    out_d = nc.dram_tensor("o", [128, NO * D], F32, kind="ExternalOutput").ap()

    ENGS = ("pe", "dve", "act", "pool", "sp")
    E = {k: [] for k in ENGS}
    cnt = {k: 0 for k in ENGS}
    last_wait = {k: {} for k in ENGS}
    sems = {}
    ctxs = []

    def mksem(name):
        cm = nc.semaphore(name)
        sems[name] = cm.__enter__()
        ctxs.append(cm)

    for s in ENGS:
        mksem(s)
    DMAS = ["s_const", "s_xown", "s_xsl0", "s_xsl1", "s_wa", "s_ws", "s_kidx",
            "s_wo", "s_bo", "s_rt", "s_oneh", "s_gu0", "s_gu1", "s_wd0",
            "s_wd1", "s_out"]
    for s in DMAS:
        mksem(s)
    dma_round = {k: 0 for k in DMAS}

    def wait(eng, token):
        if token is None:
            return
        key, val = token
        if val <= 0:
            return
        if last_wait[eng].get(key, -1) >= val:
            return
        last_wait[eng][key] = val
        E[eng].append(lambda e, k=key, v=val: e.wait_ge(sems[k], v))

    def op(eng, fn, waits=(), inc=False):
        for t in waits:
            wait(eng, t)
        if inc:
            cnt[eng] += 1
            v = cnt[eng]
            E[eng].append(lambda e, f=fn, s=eng: f(e).then_inc(sems[s], 1))
            return (eng, v)
        E[eng].append(fn)
        return None

    def opw(eng, fn, waits=()):
        t = op(eng, fn, waits=waits, inc=True)
        wait(eng, t)
        return t

    def dma(queue, sem_key, out_ap, in_ap, waits=()):
        for t in waits:
            wait(queue, t)
        dma_round[sem_key] += 1
        v = 16 * dma_round[sem_key]
        E[queue].append(lambda e, o=out_ap, i=in_ap, k=sem_key:
                        e.dma_start(out=o, in_=i).then_inc(sems[k], 16))
        return (sem_key, v)

    class Bank:
        def __init__(self, t):
            self.t = t
            self.wtok = None
            self.rtoks = []

    mm_cm = [nc.psum_tensor(f"mm{i}", [128, 512], F32) for i in range(6)]
    tp_cm = [nc.psum_tensor(f"tp{i}", [128, 1024], BF16) for i in range(2)]
    MMB = []
    for cm in mm_cm:
        ctxs.append(cm)
        MMB.append(Bank(cm.__enter__()))
    TPB = []
    for cm in tp_cm:
        ctxs.append(cm)
        TPB.append(Bank(cm.__enter__()))
    rr = {"mm": 0, "tp": 0}

    def grab(kind):
        lst = MMB if kind == "mm" else TPB
        b = lst[rr[kind] % len(lst)]
        rr[kind] += 1
        for t in b.rtoks:
            wait("pe", t)
        b.rtoks = []
        return b

    def bank_read(b, eng, tok=None):
        wait(eng, b.wtok)

    def sb(name, shape, dt):
        cm = nc.sbuf_tensor(name, list(shape), dt)
        h = cm.__enter__()
        ctxs.append(cm)
        return h

    y1 = sb("y1", [128, NO * D], F32)
    arena = sb("arena", [128, ARENA], U8)
    ar = {F32: arena.ap().bitcast(F32), BF16: arena.ap().bitcast(BF16),
          F16: arena.ap().bitcast(F16)}

    def view(byte_off, cols, dt):
        esz = 4 if dt == F32 else 2
        assert byte_off % esz == 0
        return ar[dt][:, byte_off // esz: byte_off // esz + cols]

    idbf = sb("idbf", [128, 128], BF16)
    idf = sb("idf", [128, 128], F32)
    iota16 = sb("iota16s", [128, 16], F32)
    iotaB = sb("iotaBs", [128, 16], F32)
    ones1 = sb("ones1s", [1, 128], F32)
    qidx = sb("qidxs", [128, NO], F32)
    eb_row = sb("eb_row", [1, 16], F32)
    ebias_bc = sb("ebias_bc", [128, 16], F32)
    recips = sb("recips", [128, NO], F32)
    s_tile = sb("s_tile", [128, 16], F32)
    sprime = sb("sprime", [128, 16], F32)
    msel = sb("msel", [128, 16], F32)
    wsel = sb("wsel", [128, 16], F32)
    eqm = sb("eqm", [128, 16], F32)
    tmin = sb("tmin", [128, 16], F32)
    cmb = sb("cmb", [128, NO * 16], F32)
    s1v = sb("s1v", [128, 1], F32)
    ssqv = sb("ssqv", [128, 1], F32)
    negm = sb("negm", [128, 1], F32)
    m2v = sb("m2v", [128, 1], F32)
    varv = sb("varv", [128, 1], F32)
    lnv = sb("lnv", [128, 1], F32)
    rstd = sb("rstd", [128, 1], F32)
    mxv = sb("mxv", [128, 1], F32)
    nmx = sb("nmx", [128, 1], F32)
    smr = sb("smr", [128, 1], F32)
    mx16 = sb("mx16", [128, 1], F32)
    idxm = sb("idxm", [128, 1], F32)
    dnv = sb("dnv", [128, 1], F32)
    rcpd = sb("rcpd", [128, 1], F32)
    epsv = sb("epsv", [128, 1], F32)

    def finish():
        with nc.Block() as block:
            @block.tensor
            def _(e):
                for f in E["pe"]:
                    f(e)

            @block.vector
            def _(e):
                for f in E["dve"]:
                    f(e)

            @block.scalar
            def _(e):
                for f in E["act"]:
                    f(e)

            @block.gpsimd
            def _(e):
                for f in E["pool"]:
                    f(e)

            @block.sync
            def _(e):
                for f in E["sp"]:
                    f(e)
        for cm in reversed(ctxs):
            cm.__exit__(None, None, None)
        return nc

    def dbg_dump(tag, waits, src_ap=None):
        if dbg != tag:
            return False
        ap = src_ap if src_ap is not None else y1[:, :]
        t_o = dma("sp", "s_out", out_d, ap, waits=list(waits))
        wait("sp", t_o)
        return True

    arena_tail = ar[F32][:, (ARENA - 65536) // 4: ARENA // 4]
    arena_head = ar[F32][:, 0:16384]

    for name, dst in [("ident_bf", idbf), ("ident_f32", idf), ("iota16", iota16),
                      ("iotaB16", iotaB), ("ones1", ones1), ("qidx", qidx),
                      ("ebias_w", eb_row)]:
        t_const = dma("sp", "s_const", dst[:, :], dx[name])
    t_xown = dma("sp", "s_xown", y1[:, :], dx["xown"])
    t_eps = opw("dve", lambda e: e.memset(epsv[:, :], 1e-5))

    if dbg_dump("xown", [t_xown]):
        return finish()

    V = view(0, NT * D, BF16)
    sqscr = view(65536, D, F32)
    OT = view(65536, 16 * KB, BF16)
    stateT = view(98304, 2 * KB, BF16)
    actionT = view(102400, 2 * D, BF16)
    xlnT = view(110592, 16 * RQ, BF16)
    xsl = [view(118784, D, F32), view(126976, D, F32)]
    stage = view(118784, D, BF16)
    wa_sb = view(135168, 16 * RQ, BF16)
    ws_sb = view(135168, 16 * RQ, BF16)

    t_wa = dma("pool", "s_wa", wa_sb, dx["wa_w"])

    ln_prev = [None]

    def layer_norm(src_ap, bf_dst_ap, f32_dst_ap, waits, scratch=None):
        scr = sqscr if scratch is None else scratch
        a1 = op("act", lambda e, s=src_ap, sc=scr: e.activation(
            sc, s, AF.Square, accum_out=ssqv[:, 0:1]),
            waits=list(waits) + [ln_prev[0]], inc=True)
        a2 = op("act", lambda e, s=src_ap: e.activation(
            s, s, AF.Identity, accum_out=s1v[:, 0:1]), inc=True)
        op("dve", lambda e: e.tensor_scalar(
            negm[:, :], s1v[:, :], -1.0 / D, None, ALU.mult), waits=[a2])
        opw("dve", lambda e: e.tensor_tensor(
            m2v[:, :], s1v[:, :], s1v[:, :], ALU.mult))
        d_var = op("dve", lambda e: e.scalar_tensor_tensor(
            varv[:, :], m2v[:, :], -1.0 / D, ssqv[:, :], ALU.mult, ALU.add),
            waits=[a1], inc=True)
        ln_prev[0] = d_var
        a_ln = op("act", lambda e: e.activation(
            lnv[:, :], varv[:, :], AF.Ln, bias=epsv[:, 0:1], scale=1.0 / D),
            waits=[d_var, t_eps], inc=True)
        wait("act", a_ln)
        a_r = op("act", lambda e: e.activation(
            rstd[:, :], lnv[:, :], AF.Exp, scale=-0.5), inc=True)
        toks = []
        for dst, dt in ((bf_dst_ap, "bf"), (f32_dst_ap, "f32")):
            if dst is None:
                continue
            toks.append(op("dve", lambda e, s=src_ap, d=dst: e.tensor_scalar(
                d, s, negm[:, 0:1], rstd[:, 0:1], ALU.add, ALU.mult),
                waits=[a_r], inc=True))
        return toks[0], a2

    def transpose_tile16(src_ap_fn, dst_ap_fn, wait_toks, dt):
        te = None
        ident = idbf if dt == BF16 else idf
        per = 8 if dt == BF16 else 4
        kind = "tp" if dt == BF16 else "mm"
        for blk in range(16 // per):
            b = grab(kind)
            tokp = None
            for k in range(per):
                h = blk * per + k
                tokp = op("pe", lambda e, bb=b, kk=k, hh=h:
                          e.transpose(bb.t[:, kk * 128:(kk + 1) * 128],
                                      src_ap_fn(hh), ident[:, :]),
                          waits=(list(wait_toks) + [t_const]) if h == 0 else [],
                          inc=(k == per - 1))
            b.wtok = tokp
            bank_read(b, "dve")
            te = op("dve", lambda e, bb=b, d=dst_ap_fn(blk * per, per), pp=per:
                    e.tensor_copy(d, bb.t[:, 0:pp * 128].rearrange(
                        "p (n q) -> p n q", q=128)), inc=True)
            b.rtoks.append(te)
        return te

    xslot_free = [None, None]
    xlnT_tok = [None, None]
    t_action_last = None
    for g2 in range(8):
        for tl in range(2):
            t = 2 * g2 + tl
            slot = xsl[t % 2]
            tok_dma = dma("sp", f"s_xsl{t % 2}", slot, dx["xall"][t],
                          waits=[xslot_free[t % 2]])
            d_v, a2t = layer_norm(slot, V[:, t * D:(t + 1) * D], None, [tok_dma])
            xslot_free[t % 2] = d_v
            xlnT3 = ar[BF16][:, 110592 // 2: 110592 // 2 + 16 * RQ].rearrange(
                "p (h q) -> p h q", q=RQ)
            te = transpose_tile16(
                lambda h, t_=t: V[:, t_ * D + h * 128: t_ * D + (h + 1) * 128],
                lambda lo, n, tl_=tl: xlnT3[:, lo:lo + n, tl_ * 128:(tl_ + 1) * 128],
                [d_v], BF16)
            xlnT_tok[tl] = te
        for rc in range(2):
            b = grab("mm")
            tokp = None
            for h in range(16):
                tokp = op("pe", lambda e, bb=b, h_=h, rc_=rc:
                          e.matmul(bb.t[:, 0:RQ],
                                   wa_sb[:, h_ * RQ + rc_ * 128: h_ * RQ + rc_ * 128 + 128],
                                   xlnT[:, h_ * RQ: h_ * RQ + RQ],
                                   start=(h_ == 0), stop=(h_ == 15)),
                          waits=([xlnT_tok[0], xlnT_tok[1], t_wa] if h == 0 else ()),
                          inc=(h == 15))
            b.wtok = tokp
            t_action_last = tokp
            bank_read(b, "dve")
            ta = op("dve", lambda e, bb=b, rc_=rc, g_=g2: e.tensor_copy(
                actionT[:, rc_ * D + g_ * RQ: rc_ * D + (g_ + 1) * RQ],
                bb.t[:, 0:RQ]), inc=True)
            b.rtoks.append(ta)
            dve_action_tok = ta

    if dbg_dump("V", [dve_action_tok], arena_head):
        return finish()

    t_ws = dma("pool", "s_ws", ws_sb, dx["ws_w"], waits=[t_action_last])
    t_state_last = None
    for g2 in range(4):
        for tl in range(2):
            s_ = 2 * g2 + tl
            d_v, _ = layer_norm(y1[:, s_ * D:(s_ + 1) * D], stage, None,
                                [t_xown, xslot_free[0], xslot_free[1]])
            xslot_free[0] = d_v
            xlnT3 = ar[BF16][:, 110592 // 2: 110592 // 2 + 16 * RQ].rearrange(
                "p (h q) -> p h q", q=RQ)
            te = transpose_tile16(
                lambda h: stage[:, h * 128:(h + 1) * 128],
                lambda lo, n, tl_=tl: xlnT3[:, lo:lo + n, tl_ * 128:(tl_ + 1) * 128],
                [d_v], BF16)
            xlnT_tok[tl] = te
        for rc in range(2):
            b = grab("mm")
            tokp = None
            for h in range(16):
                tokp = op("pe", lambda e, bb=b, h_=h, rc_=rc:
                          e.matmul(bb.t[:, 0:RQ],
                                   ws_sb[:, h_ * RQ + rc_ * 128: h_ * RQ + rc_ * 128 + 128],
                                   xlnT[:, h_ * RQ: h_ * RQ + RQ],
                                   start=(h_ == 0), stop=(h_ == 15)),
                          waits=([xlnT_tok[0], xlnT_tok[1], t_ws] if h == 0 else ()),
                          inc=(h == 15))
            b.wtok = tokp
            t_state_last = tokp
            bank_read(b, "dve")
            ta = op("dve", lambda e, bb=b, rc_=rc, g_=g2: e.tensor_copy(
                stateT[:, rc_ * KB + g_ * RQ: rc_ * KB + (g_ + 1) * RQ],
                bb.t[:, 0:RQ]), inc=True)
            b.rtoks.append(ta)
            dve_state_tok = ta

    if dbg_dump("sa", [dve_state_tok], arena_tail):
        return finish()

    S_f32 = view(110592, D, F32)
    PT = view(118784, 16 * 512, BF16)
    kidx16 = view(135168, D, F16)
    tmsk = view(139264, 512, F32)
    S_bf = view(139264, D, BF16)

    t_kidx = dma("sp", "s_kidx", kidx16, dx["kidx"], waits=[t_state_last])

    pe_pt_done = None
    pe_av_done = None
    for G in range(2):
        pt_tok = None
        for sp_ in range(4):
            s_ = 4 * G + sp_
            kend = 1152 + 128 * s_
            nkc = kend // 128
            k0 = 0
            ev_tok = None
            while k0 < kend:
                w_ = min(512, kend - k0)
                b = grab("mm")
                op("pe", lambda e, bb=b, k0_=k0, w2=w_, s2=s_:
                   e.matmul(bb.t[:, 0:w2],
                            stateT[:, s2 * 128:(s2 + 1) * 128],
                            actionT[:, k0_:k0_ + w2],
                            start=True, stop=False),
                   waits=[dve_state_tok, dve_action_tok], inc=False)
                tk2 = op("pe", lambda e, bb=b, k0_=k0, w2=w_, s2=s_:
                         e.matmul(bb.t[:, 0:w2],
                                  stateT[:, KB + s2 * 128: KB + (s2 + 1) * 128],
                                  actionT[:, D + k0_: D + k0_ + w2],
                                  start=False, stop=True), inc=True)
                b.wtok = tk2
                pe_s_done = tk2
                op("dve", lambda e, k0_=k0, w2=w_, s2=s_: e.tensor_scalar(
                    tmsk[:, 0:w2], kidx16[:, k0_:k0_ + w2], qidx[:, s2:s2 + 1],
                    NEG, ALU.is_gt, ALU.mult), waits=[t_kidx, t_const])
                bank_read(b, "dve")
                ev_tok = op("dve", lambda e, bb=b, k0_=k0, w2=w_: e.tensor_tensor(
                    S_f32[:, k0_:k0_ + w2], tmsk[:, 0:w2], bb.t[:, 0:w2], ALU.add),
                    waits=[t_state_last], inc=True)
                b.rtoks.append(ev_tok)
                k0 += w_
            if G == 0 and sp_ == 0 and dbg_dump("sraw", [ev_tok], arena_tail):
                return finish()
            opw("dve", lambda e, ke=kend: e.reduce_max(
                mxv[:, :], S_f32[:, 0:ke], AX.X), waits=[ev_tok])
            d_nmx = opw("dve", lambda e: e.tensor_scalar(
                nmx[:, :], mxv[:, :], -1.0 / 16.0, None, ALU.mult))
            a_exp = op("act", lambda e, ke=kend: e.activation(
                S_bf[:, 0:ke], S_f32[:, 0:ke], AF.Exp, bias=nmx[:, 0:1],
                scale=1.0 / 16.0, accum_out=smr[:, 0:1]),
                waits=[d_nmx], inc=True)
            d_rc = opw("dve", lambda e, s2=s_: e.reciprocal(
                recips[:, s2:s2 + 1], smr[:, 0:1]), waits=[a_exp])
            if G == 0 and sp_ == 0 and dbg == "s0":
                op("dve", lambda e: e.tensor_copy(S_f32[:, 2000:2001], mxv[:, :]),
                   waits=[a_exp])
                op("dve", lambda e: e.tensor_copy(S_f32[:, 2001:2002], nmx[:, :]))
                op("dve", lambda e: e.tensor_copy(S_f32[:, 2002:2003], smr[:, :]))
                if dbg_dump("s0", [d_rc, a_exp, ("dve", cnt["dve"])], arena_tail):
                    return finish()
            PT3 = ar[BF16][:, 118784 // 2: 118784 // 2 + 16 * 512].rearrange(
                "p (k q) -> p k q", q=512)
            blocks = (nkc + 7) // 8
            for blk in range(blocks):
                per = min(8, nkc - blk * 8)
                b = grab("tp")
                tokp = None
                for k in range(per):
                    kc = blk * 8 + k
                    tokp = op("pe", lambda e, bb=b, kk=k, kc_=kc:
                              e.transpose(bb.t[:, kk * 128:(kk + 1) * 128],
                                          S_bf[:, kc_ * 128:(kc_ + 1) * 128],
                                          idbf[:, :]),
                              waits=[a_exp] if k == 0 else (), inc=(k == per - 1))
                b.wtok = tokp
                pt_tok = tokp
                bank_read(b, "dve")
                te = op("dve", lambda e, bb=b, d=PT3[:, blk * 8: blk * 8 + per,
                                                     sp_ * 128:(sp_ + 1) * 128],
                        pp=per: e.tensor_copy(
                            d, bb.t[:, 0:pp * 128].rearrange("p (n q) -> p n q", q=128)),
                        inc=True)
                b.rtoks.append(te)
                pt_ev = te
            if G == 0 and sp_ == 0 and dbg_dump("pt0", [pt_ev], arena_tail):
                return finish()
        pe_pt_done = pt_tok
        nkc_max = 12 + 4 * G
        for h in range(16):
            b = grab("mm")
            tokp = None
            for kc in range(nkc_max):
                sp0 = max(0, kc - 8 - 4 * G)
                wq = 512 - sp0 * 128
                tokp = op("pe", lambda e, bb=b, kc_=kc, sp0_=sp0, wq_=wq, h_=h,
                          sp9=(kc == nkc_max - 1):
                          e.matmul(bb.t[:, sp0_ * 128: sp0_ * 128 + wq_],
                                   V[:, kc_ * D + h_ * 128: kc_ * D + (h_ + 1) * 128],
                                   PT[:, kc_ * 512 + sp0_ * 128: (kc_ + 1) * 512],
                                   start=(kc_ == 0), stop=sp9),
                          waits=[pt_ev] if kc == 0 else (), inc=(kc == nkc_max - 1))
            b.wtok = tokp
            pe_av_done = tokp
            bank_read(b, "dve")
            te = op("dve", lambda e, bb=b, h_=h, G_=G: e.tensor_copy(
                OT[:, h_ * KB + G_ * 512: h_ * KB + (G_ + 1) * 512], bb.t[:, :]),
                inc=True)
            b.rtoks.append(te)
            dve_OT_tok = te

    if dbg_dump("ot", [dve_OT_tok],
                ar[F32][:, 65536 // 4: 65536 // 4 + 16384]):
        return finish()

    WoT = view(0, 16 * D, BF16)
    bo_bc = view(98304, D, F32)
    bo_row = ar[F32][0:1, 118784 // 4: 118784 // 4 + D]

    t_wo = [dma("pool", "s_wo", WoT[:, j * 8192:(j + 1) * 8192],
                dx["wo_w"][:, j * 8192:(j + 1) * 8192],
                waits=[pe_av_done] if j == 0 else ()) for j in range(4)]
    t_bo = dma("sp", "s_bo", bo_row, dx["bo_w"], waits=[pe_pt_done, pe_av_done])

    bo_ev = []
    for j in range(4):
        b = grab("mm")
        tokp = op("pe", lambda e, bb=b, j_=j: e.matmul(
            bb.t[:, :], ones1[0:1, :], bo_row[0:1, j_ * 512:(j_ + 1) * 512],
            start=True, stop=True), waits=[t_bo, t_const], inc=True)
        b.wtok = tokp
        bank_read(b, "dve")
        te = op("dve", lambda e, bb=b, j_=j: e.tensor_copy(
            bo_bc[:, j_ * 512:(j_ + 1) * 512], bb.t[:, :]),
            waits=[pe_s_done], inc=True)
        b.rtoks.append(te)
        bo_ev.append(te)

    dve_y1_tok = None
    for s_ in range(NO):
        for os_ in range(4):
            b = grab("mm")
            tokp = None
            for h in range(16):
                tokp = op("pe", lambda e, bb=b, h_=h, s2=s_, o2=os_:
                          e.matmul(bb.t[:, :],
                                   OT[:, h_ * KB + s2 * 128: h_ * KB + (s2 + 1) * 128],
                                   WoT[:, h_ * D + o2 * 512: h_ * D + (o2 + 1) * 512],
                                   start=(h_ == 0), stop=(h_ == 15)),
                          waits=[("s_wo", 64), dve_OT_tok] if h == 0 else (),
                          inc=(h == 15))
            b.wtok = tokp
            pe_wo_done = tokp
            bank_read(b, "dve")
            te = op("dve", lambda e, bb=b, s2=s_, o2=os_: e.scalar_tensor_tensor(
                y1[:, s2 * D + o2 * 512: s2 * D + (o2 + 1) * 512],
                bb.t[:, :], recips[:, s2:s2 + 1],
                y1[:, s2 * D + o2 * 512: s2 * D + (o2 + 1) * 512],
                ALU.mult, ALU.add), inc=True)
            b.rtoks.append(te)
            dve_y1_tok = te
    for t_ in bo_ev:
        wait("dve", t_)
    for s_ in range(NO):
        dve_y1_tok = op("dve", lambda e, s2=s_: e.tensor_tensor(
            y1[:, s2 * D:(s2 + 1) * D], y1[:, s2 * D:(s2 + 1) * D],
            bo_bc[:, :], ALU.add), waits=bo_ev, inc=True)

    if dbg_dump("y1a", [dve_y1_tok]):
        return finish()

    xln2T = view(65536, 16 * KB, BF16)
    scr32 = view(0, 4 * 2048, F32)
    xln2f = view(32768, D, F32)
    rt_sb = view(40960, 16 * 16, F32)

    t_rt = dma("pool", "s_rt", rt_sb, dx["router_w"], waits=[pe_wo_done])

    b = grab("mm")
    tokp = op("pe", lambda e, bb=b: e.matmul(
        bb.t[:, 0:16], ones1[0:1, :], eb_row[0:1, :], start=True, stop=True),
        waits=[t_const], inc=True)
    b.wtok = tokp
    bank_read(b, "dve")
    te = op("dve", lambda e, bb=b: e.tensor_copy(ebias_bc[:, :], bb.t[:, 0:16]),
            inc=True)
    b.rtoks.append(te)
    wait("dve", te)

    pe_b0_done = None
    dve_c_tok = None
    scr3 = ar[F32][:, 0: 4 * 2048].rearrange("p (h q) -> p h q", q=512)
    xl3 = ar[BF16][:, 65536 // 2: 65536 // 2 + 16 * KB].rearrange(
        "p (h q) -> p h q", q=KB)
    for grp in range(2):
        for tl in range(4):
            s_ = 4 * grp + tl
            d_v, _ = layer_norm(y1[:, s_ * D:(s_ + 1) * D], None, xln2f,
                                [dve_y1_tok, pe_wo_done], scratch=xln2f)
            te = None
            for blk in range(4):
                b = grab("mm")
                tokp = None
                for k in range(4):
                    h = blk * 4 + k
                    tokp = op("pe", lambda e, bb=b, kk=k, hh=h:
                              e.transpose(bb.t[:, kk * 128:(kk + 1) * 128],
                                          xln2f[:, hh * 128:(hh + 1) * 128],
                                          idf[:, :]),
                              waits=[d_v] if k == 0 else (), inc=(k == 3))
                b.wtok = tokp
                pe_b0_done = tokp
                bank_read(b, "dve")
                t_f = op("dve", lambda e, bb=b, d=scr3[:, blk * 4: blk * 4 + 4,
                                                       tl * 128:(tl + 1) * 128]:
                         e.tensor_copy(d, bb.t[:, :].rearrange(
                             "p (n q) -> p n q", q=128)), inc=True)
                t_b = op("dve", lambda e, bb=b, d=xl3[:, blk * 4: blk * 4 + 4,
                                                      s_ * 128:(s_ + 1) * 128]:
                         e.tensor_copy(d, bb.t[:, :].rearrange(
                             "p (n q) -> p n q", q=128)), inc=True)
                b.rtoks += [t_f, t_b]
                te = t_f
                dve_xl2_tok = t_b
            b = grab("mm")
            tokp = None
            for h in range(16):
                tokp = op("pe", lambda e, bb=b, h_=h, tl_=tl:
                          e.matmul(bb.t[:, 0:16],
                                   scr3[:, h_, tl_ * 128:(tl_ + 1) * 128],
                                   rt_sb[:, h_ * 16:(h_ + 1) * 16],
                                   start=(h_ == 0), stop=(h_ == 15)),
                          waits=[te, t_rt] if h == 0 else (), inc=(h == 15))
            b.wtok = tokp
            pe_b0_done = tokp
            bank_read(b, "act")
            a_sig = op("act", lambda e, bb=b: e.activation(
                s_tile[:, :], bb.t[:, 0:16], AF.Sigmoid), inc=True)
            b.rtoks.append(a_sig)
            if dbg == "scores":
                dve_c_tok = op("dve", lambda e, s2=s_: e.tensor_copy(
                    cmb[:, s2 * 16:(s2 + 1) * 16], s_tile[:, :]),
                    waits=[a_sig], inc=True)
                continue
            opw("dve", lambda e: e.tensor_tensor(
                sprime[:, :], s_tile[:, :], ebias_bc[:, :], ALU.add),
                waits=[a_sig, te])
            opw("dve", lambda e: e.memset(msel[:, :], 0.0))
            for rnd in range(4):
                opw("dve", lambda e: e.reduce_max(mx16[:, :], sprime[:, :], AX.X))
                opw("dve", lambda e: e.tensor_scalar(
                    eqm[:, :], sprime[:, :], mx16[:, 0:1], None, ALU.is_equal))
                opw("dve", lambda e: e.scalar_tensor_tensor(
                    tmin[:, :], eqm[:, :], -1024.0, iotaB[:, :], ALU.mult, ALU.add))
                opw("dve", lambda e: e.tensor_reduce(
                    idxm[:, :], tmin[:, :], AX.X, ALU.min))
                opw("dve", lambda e: e.tensor_scalar(
                    eqm[:, :], iota16[:, :], idxm[:, 0:1], None, ALU.is_equal))
                opw("dve", lambda e: e.tensor_tensor(
                    msel[:, :], msel[:, :], eqm[:, :], ALU.add))
                opw("dve", lambda e: e.scalar_tensor_tensor(
                    sprime[:, :], eqm[:, :], NEG, sprime[:, :], ALU.mult, ALU.add))
            opw("dve", lambda e: e.tensor_tensor(
                wsel[:, :], msel[:, :], s_tile[:, :], ALU.mult))
            opw("dve", lambda e: e.tensor_reduce(
                dnv[:, :], wsel[:, :], AX.X, ALU.add))
            opw("dve", lambda e: e.tensor_scalar(
                dnv[:, :], dnv[:, :], 1e-20, None, ALU.add))
            opw("dve", lambda e: e.reciprocal(rcpd[:, :], dnv[:, :]))
            dve_c_tok = op("dve", lambda e, s2=s_: e.tensor_scalar(
                cmb[:, s2 * 16:(s2 + 1) * 16], wsel[:, :], rcpd[:, 0:1], None,
                ALU.mult), inc=True)

    cT = view(133120, KB, F32)
    cT_full = ar[F32][0:16, 133120 // 4: 133120 // 4 + KB]
    for s_ in range(NO):
        b = grab("mm")
        tokp = op("pe", lambda e, bb=b, s2=s_: e.transpose(
            bb.t[0:16, 0:128], cmb[:, s2 * 16:(s2 + 1) * 16], idf[:, :]),
            waits=[dve_c_tok], inc=True)
        b.wtok = tokp
        pe_b0_done = tokp
        bank_read(b, "dve")
        te = op("dve", lambda e, bb=b, s2=s_: e.tensor_copy(
            cT_full[:, s2 * 128:(s2 + 1) * 128], bb.t[0:16, 0:128]), inc=True)
        b.rtoks.append(te)
        dve_cT_tok = te

    if dbg in ("cmb", "scores"):
        t_o = dma("sp", "s_out", out_d[:, 0:NO * 16],
                  cmb[:, :], waits=[dve_c_tok])
        wait("sp", t_o)
        return finish()
    if dbg_dump("xl2", [dve_xl2_tok],
                ar[F32][:, 65536 // 4: 65536 // 4 + 16384]):
        return finish()

    gu_sl = [view(0, 4096, BF16), view(8192, 4096, BF16)]
    wd_sb = [view(16384, 6 * D, BF16), view(40960, 6 * D, BF16)]
    a_sb = view(98304, 12 * KB, BF16)
    cbc = [view(122880, KB, F32), view(126976, KB, F32)]
    t1 = view(131072, 512, F32)
    oneh = ar[F32][0:16, 137216 // 4: 137216 // 4 + 128]

    pairs = [(0, 1), (2, 3), (4, 5), (6, 7), (8, 9), (10, 11), (12, 13),
             (14, 15), (16,)]
    gu_used = {0: None, 1: None}
    t1_free = [None]
    wd_used = [None, None]
    jglob = [0]
    pe_down_done = None
    dve_down_tok = None

    for pi, pair in enumerate(pairs):
        prev_down = pe_down_done
        t_wd = []
        for ei, e_ in enumerate(pair):
            t_wd.append(dma("pool", f"s_wd{ei}", wd_sb[ei], dx["wd_w"][e_],
                            waits=[wd_used[ei],
                                   pe_b0_done if pi == 0 else None]))
        prev_cbc_mm = [pe_b0_done if pi == 0 else pe_down_done]
        for ei, e_ in enumerate(pair):
            if e_ == 16:
                continue
            t_oneh = dma("pool", "s_oneh", oneh,
                         dx["onehots"][:, e_ * 128:(e_ + 1) * 128],
                         waits=prev_cbc_mm)
            for st_ in range(2):
                b = grab("mm")
                tokp = op("pe", lambda e, bb=b, st2=st_: e.matmul(
                    bb.t[:, :], oneh[:, :], cT_full[:, st2 * 512:(st2 + 1) * 512],
                    start=True, stop=True),
                    waits=[t_oneh, dve_cT_tok], inc=True)
                b.wtok = tokp
                prev_cbc_mm = [tokp]
                bank_read(b, "dve")
                te = op("dve", lambda e, bb=b, ei_=ei, st2=st_: e.tensor_copy(
                    cbc[ei_][:, st2 * 512:(st2 + 1) * 512], bb.t[:, :]), inc=True)
                b.rtoks.append(te)
                cbc_tok = te
                wait("dve", te)
        if pi == 0 and dbg == "d0w":
            for tt in t_wd:
                wait("sp", tt)
            if dbg_dump("d0w", t_wd, arena_head):
                return finish()
        for ei, e_ in enumerate(pair):
            for j in range(6):
                par = jglob[0] % 2
                jglob[0] += 1
                t_gu = dma("pool", f"s_gu{par}", gu_sl[par], dx["gu_w"][e_, j],
                           waits=[gu_used[par],
                                  pe_b0_done if pi == 0 and jglob[0] <= 2 else None])
                for st_ in range(2):
                    gb = grab("mm")
                    ub = grab("mm")
                    pair_b = (gb, ub)
                    tokp = None
                    for h in range(16):
                        for is_ in range(2):
                            bb = pair_b[is_]
                            tokp = op("pe", lambda e, b2=bb, h_=h, i2=is_, s2=st_,
                                      p2=par: e.matmul(
                                          b2.t[:, :],
                                          gu_sl[p2][:, (h_ * 2 + i2) * 128:
                                                    (h_ * 2 + i2 + 1) * 128],
                                          xln2T[:, h_ * KB + s2 * 512:
                                                h_ * KB + (s2 + 1) * 512],
                                          start=(h_ == 0), stop=(h_ == 15)),
                                      waits=[t_gu, dve_xl2_tok] if (h == 0 and
                                             is_ == 0) else (),
                                      inc=(h == 15))
                    gb.wtok = tokp
                    ub.wtok = tokp
                    gu_used[par] = tokp
                    bank_read(gb, "act")
                    a_si = op("act", lambda e, b2=gb: e.activation(
                        t1[:, :], b2.t[:, :], AF.Silu),
                        waits=[t1_free[0]], inc=True)
                    gb.rtoks.append(a_si)
                    bank_read(ub, "dve")
                    d_m1 = op("dve", lambda e, b2=ub: e.tensor_tensor(
                        t1[:, :], t1[:, :], b2.t[:, :], ALU.mult),
                        waits=[a_si, prev_down], inc=True)
                    ub.rtoks.append(d_m1)
                    dst = a_sb[:, (ei * 6 + j) * KB + st_ * 512:
                               (ei * 6 + j) * KB + (st_ + 1) * 512]
                    if e_ == 16:
                        d_a = op("dve", lambda e, d2=dst: e.tensor_scalar(
                            d2, t1[:, :], 1.0, None, ALU.mult), inc=True)
                    else:
                        d_a = op("dve", lambda e, d2=dst, ei_=ei, s2=st_:
                                 e.tensor_tensor(d2, t1[:, :],
                                                 cbc[ei_][:, s2 * 512:(s2 + 1) * 512],
                                                 ALU.mult),
                                 waits=[cbc_tok], inc=True)
                    t1_free[0] = d_a
                    a_tok = d_a
        if pi == 0 and dbg_dump("a0", [a_tok, cbc_tok], arena_tail):
            return finish()
        nacc = len(pair) * 6
        for s_ in range(NO):
            for half in range(2):
                banks = [grab("mm"), grab("mm")]
                tokp = None
                for acc_i in range(nacc):
                    ei, i_ = acc_i // 6, acc_i % 6
                    for oh in range(2):
                        os_ = 2 * half + oh
                        bb = banks[oh]
                        if dbg == "d0x" and pi == 0:
                            lhs_ap = wd_sb[0][:, 0:128]
                        else:
                            lhs_ap = a_sb[:, (ei * 6 + i_) * KB + s_ * 128:
                                          (ei * 6 + i_) * KB + (s_ + 1) * 128]
                        tokp = op("pe", lambda e, b2=bb, la=lhs_ap, ei_=ei, i2=i_,
                                  o2=os_, st0=(acc_i == 0), sp0=(acc_i == nacc - 1):
                                  e.matmul(
                                      b2.t[:, :], la,
                                      wd_sb[ei_][:, i2 * D + o2 * 512:
                                                 i2 * D + (o2 + 1) * 512],
                                      start=st0, stop=sp0),
                                  waits=([a_tok] + t_wd) if (acc_i == 0 and oh == 0)
                                        else (), inc=(acc_i == nacc - 1))
                for bb in banks:
                    bb.wtok = tokp
                pe_down_done = tokp
                for ei in range(len(pair)):
                    wd_used[ei] = tokp
                for oh in range(2):
                    os_ = 2 * half + oh
                    bb = banks[oh]
                    bank_read(bb, "dve")
                    if dbg in ("d0copy", "d0x") and pi == 0:
                        te = op("dve", lambda e, b2=bb, s2=s_, o2=os_: e.tensor_copy(
                            y1[:, s2 * D + o2 * 512: s2 * D + (o2 + 1) * 512],
                            b2.t[:, :]), inc=True)
                    else:
                        te = op("dve", lambda e, b2=bb, s2=s_, o2=os_: e.tensor_tensor(
                            y1[:, s2 * D + o2 * 512: s2 * D + (o2 + 1) * 512],
                            y1[:, s2 * D + o2 * 512: s2 * D + (o2 + 1) * 512],
                            b2.t[:, :], ALU.add), inc=True)
                    bb.rtoks.append(te)
                    dve_down_tok = te
        if pi == 0 and dbg in ("d0", "d0copy", "d0x") and dbg_dump(dbg, [dve_down_tok]):
            return finish()
        if pi == 0 and dbg == "d0a" and dbg_dump("d0a", [dve_down_tok, a_tok],
                                                 arena_tail):
            return finish()

    if dbg == "d0pair":
        pass
    t_out = dma("sp", "s_out", out_d, y1[:, :], waits=[dve_down_tok])
    wait("sp", t_out)

    return finish()



def _slabify(w):
    C = w.shape[1]
    return np.ascontiguousarray(
        w.reshape(16, 128, C).transpose(1, 0, 2).reshape(128, 16 * C))


def kernel(L0_attn_x, L0_attn_Ws, L0_attn_Wa, L0_attn_Wo, L0_attn_bo,
           L0_ffn_router_w, L0_ffn_e_bias, L0_ffn_gate_up_proj,
           L0_ffn_down_proj, L0_ffn_shared_gate_w, L0_ffn_shared_up_w,
           L0_ffn_shared_down_w):
    x = np.asarray(L0_attn_x, np.float32)
    Ws = np.asarray(L0_attn_Ws, np.float32)
    Wa = np.asarray(L0_attn_Wa, np.float32)
    Wo = np.asarray(L0_attn_Wo, np.float32)
    bo = np.asarray(L0_attn_bo, np.float32)
    router = np.asarray(L0_ffn_router_w, np.float32)
    ebias = np.asarray(L0_ffn_e_bias, np.float32)
    gup = np.asarray(L0_ffn_gate_up_proj, np.float32)
    dwn = np.asarray(L0_ffn_down_proj, np.float32)
    sgw = np.asarray(L0_ffn_shared_gate_w, np.float32)
    suw = np.asarray(L0_ffn_shared_up_w, np.float32)
    sdw = np.asarray(L0_ffn_shared_down_w, np.float32)

    B = x.shape[0]

    ws_w = _slabify(Ws)
    wa_w = _slabify(Wa)
    wo_w = _slabify(np.ascontiguousarray(Wo.T))
    router_w = _slabify(np.ascontiguousarray(router.T))
    bo_w = bo.reshape(1, D)
    ebias_w = ebias.reshape(1, 16)

    gu_list = [gup[e] for e in range(16)] + [np.vstack([sgw, suw])]
    gu_w = np.empty((NEXP, 6, 128, 4096), np.float32)
    for e in range(NEXP):
        gu_w[e] = (gu_list[e].reshape(2, 6, 128, 16, 128)
                   .transpose(1, 4, 3, 0, 2).reshape(6, 128, 4096))
    wd_list = [dwn[e] for e in range(16)] + [sdw]
    wd_w = np.empty((NEXP, 128, 6 * D), np.float32)
    for e in range(NEXP):
        wd_w[e] = (wd_list[e].T.reshape(6, 128, D)
                   .transpose(1, 0, 2).reshape(128, 6 * D))

    onehots = np.zeros((16, 16 * 128), np.float32)
    for e in range(16):
        onehots[e, e * 128:(e + 1) * 128] = 1.0

    consts = dict(
        kidx=np.broadcast_to(np.arange(D, dtype=np.float16), (128, D)).copy(),
        ident_bf=np.eye(128, dtype=ml_dtypes.bfloat16),
        ident_f32=np.eye(128, dtype=np.float32),
        iota16=np.broadcast_to(np.arange(16, dtype=np.float32), (128, 16)).copy(),
        iotaB16=(np.broadcast_to(np.arange(16, dtype=np.float32), (128, 16)) + 1024.0
                 ).astype(np.float32),
        ones1=np.ones((1, 128), np.float32),
        onehots=onehots,
        ws_w=ws_w, wa_w=wa_w, wo_w=wo_w, bo_w=bo_w,
        router_w=router_w, ebias_w=ebias_w, gu_w=gu_w, wd_w=wd_w,
    )

    nc = build_program(dbg=DBG)
    in_maps = []
    for c in range(8):
        b, p = c // 2, c % 2
        xo = (x[b, p * 1024:(p + 1) * 1024]
              .reshape(8, 128, D).transpose(1, 0, 2).reshape(128, NO * D))
        xo = np.ascontiguousarray(xo)
        xall = np.ascontiguousarray(x[b].reshape(NT, 128, D))
        qi = (np.arange(p * 1024, (p + 1) * 1024, dtype=np.float32)
              .reshape(8, 128).T)
        qi = np.ascontiguousarray(qi)
        in_maps.append(dict(xown=xo, xall=xall, qidx=qi, **consts))

    res = run_bass_kernel_spmd(nc, in_maps, core_ids=list(range(8)))
    if DBG is not None:
        return [res.results[c]["o"] for c in range(8)]

    out = np.empty((B, D, D), np.float32)
    for c in range(8):
        b, p = c // 2, c % 2
        o = res.results[c]["o"].reshape(128, 8, D).transpose(1, 0, 2)
        out[b, p * 1024:(p + 1) * 1024] = o.reshape(1024, D)
    return out
